# version 35
# baseline (speedup 1.0000x reference)
import numpy as np
import ml_dtypes
from contextlib import ExitStack

import concourse.bass as bass
import concourse.mybir as mybir
from concourse.bass_utils import run_bass_kernel_spmd

F32 = mybir.dt.float32
BF16 = mybir.dt.bfloat16
AF = mybir.ActivationFunctionType
ALU = mybir.AluOpType

D = 1024
EPS = 1e-5
N_CORES = 8


class Kern:
    COMPUTE = ("pe", "ve", "act", "po")

    def __init__(self, nc, ctx):
        self.nc = nc
        self.sems = {}
        self.cnt = {}
        for name in ("pe", "ve", "act", "po"):
            self.sems[name] = ctx.enter_context(nc.semaphore("S_" + name))
            self.cnt[name] = 0
        for q in ("sp", "gp"):
            for i in range(4):
                n = f"{q}{i}"
                self.sems[n] = ctx.enter_context(nc.semaphore("S_" + n))
                self.cnt[n] = 0
        self.rr = {"sp": 0, "gp": 0}
        self.stream = {e: [] for e in ("pe", "ve", "act", "po", "sp", "gp")}
        self.waited = {e: {} for e in self.stream}
        self.regions = {}

    def _engine_stream(self, eng):
        return "gp" if eng == "po" else eng

    def _enqueue(self, eng, deps, fn, inc_sem, inc_val):
        est = self._engine_stream(eng)
        need = {}
        for (s, v) in deps:
            if v > need.get(s, 0):
                need[s] = v
        waited = self.waited[est]
        waits = []
        for s, v in sorted(need.items()):
            if s == "pe" and est == "pe":
                continue
            if waited.get(s, 0) < v:
                waits.append((s, v))
                waited[s] = v
        self.stream[est].append((waits, fn, inc_sem, inc_val))
        return (inc_sem, self.cnt[inc_sem])

    def _deps_and_update(self, reads, writes, tok_fn):
        deps = []
        for r in reads:
            reg = self.regions.get(r)
            if reg:
                deps += reg["w"]
        for w in writes:
            reg = self.regions.get(w)
            if reg:
                deps += reg["r"] + reg["w"]
        tok = tok_fn(deps)
        for r in reads:
            self.regions.setdefault(r, {"w": [], "r": []})["r"].append(tok)
        for w in writes:
            self.regions[w] = {"w": [tok], "r": []}
        return tok

    def op(self, eng, reads, writes, fn):
        def go(deps):
            self.cnt[eng] += 1
            return self._enqueue(eng, deps, fn, eng, 1)
        return self._deps_and_update(reads, writes, go)

    def dma(self, queue, reads, writes, fn):
        def go(deps):
            i = self.rr[queue]
            self.rr[queue] = (i + 1) % 4
            sem = f"{queue}{i}"
            if self.cnt[sem] > 0:
                deps = deps + [(sem, self.cnt[sem])]
            self.cnt[sem] += 16
            return self._enqueue(queue, deps, fn, sem, 16)
        return self._deps_and_update(reads, writes, go)

    def finalize(self, block):
        engines = {
            "pe": block.tensor, "ve": block.vector, "act": block.scalar,
            "sp": block.sync, "gp": block.gpsimd,
        }
        for est, section in engines.items():
            stream = self.stream[est]
            sems = self.sems

            def emit(engobj, stream=stream, sems=sems):
                for waits, fn, inc_sem, inc_val in stream:
                    for s, v in waits:
                        engobj.wait_ge(sems[s], v)
                    ins = fn(engobj)
                    ins.then_inc(sems[inc_sem], inc_val)

            section(emit)


def build_program(n_seq=2, S=4096):
    CF = S // 1024
    NT = S // 128
    NG = S // 512
    NQG = S // 256
    NKTH = S // 256
    NMT = CF * 8
    VH = S // 256

    nc = bass.Bass("TRN2", target_bir_lowering=False)

    x_in = nc.dram_tensor("x", [n_seq * S, D], F32, kind="ExternalInput").ap()
    wqT_in = nc.dram_tensor("wqT", [D, 3 * D], F32, kind="ExternalInput").ap()
    wpT_in = nc.dram_tensor("wpT", [D, D], F32, kind="ExternalInput").ap()
    bpj_in = nc.dram_tensor("bpj", [1, D], F32, kind="ExternalInput").ap()
    bfn_in = nc.dram_tensor("bfn", [1, D], F32, kind="ExternalInput").ap()
    atR_in = nc.dram_tensor("atR", [32, 128], F32, kind="ExternalInput").ap()
    atI_in = nc.dram_tensor("atI", [32, 128], F32, kind="ExternalInput").ap()
    btR_in = nc.dram_tensor("btR", [16, 64], F32, kind="ExternalInput").ap()
    btI_in = nc.dram_tensor("btI", [16, 64], F32, kind="ExternalInput").ap()
    U_in = nc.dram_tensor("Udel", [32, 512], F32, kind="ExternalInput").ap()
    T_in = nc.dram_tensor("Tdel", [16, 512], F32, kind="ExternalInput").ap()
    id_in = nc.dram_tensor("ident", [128, 128], BF16, kind="ExternalInput").ap()
    orow_in = nc.dram_tensor("orow", [1, 128], BF16, kind="ExternalInput").ap()
    mrow_in = nc.dram_tensor("morow", [1, 128], BF16, kind="ExternalInput").ap()
    ocol_in = nc.dram_tensor("ocol", [128, 1], BF16, kind="ExternalInput").ap()
    out_d = nc.dram_tensor("out", [n_seq * S, D], F32, kind="ExternalOutput").ap()

    x1d = nc.dram_tensor("x1d", [n_seq * S, D], F32, kind="Internal").ap()
    xn2d = nc.dram_tensor("xn2d", [8, 128, n_seq * S], BF16, kind="Internal").ap()
    qd = nc.dram_tensor("qd", [n_seq, 8, 128, S], BF16, kind="Internal").ap()
    kd = nc.dram_tensor("kd", [n_seq, 8, 128, S], BF16, kind="Internal").ap()
    vd = nc.dram_tensor("vd", [n_seq, NT, 128, D], BF16, kind="Internal").ap()
    od = nc.dram_tensor("od", [n_seq * S, D], BF16, kind="Internal").ap()
    opd = nc.dram_tensor("opd", [S, D], F32, kind="Internal").ap()

    with ExitStack() as ctx:
        def sb(name, shape, dt):
            return ctx.enter_context(nc.sbuf_tensor(name, shape, dt))

        bigbuf = sb("bigbuf", [128, 32768], BF16)
        Wc = sb("Wc", [128, 8 * D], BF16)
        wpT = sb("wpTb", [128, 8 * D], BF16)
        wstage = sb("wstage", [128, 3072], F32)
        xt = [sb(f"xt{i}", [128, D], F32) for i in range(2)]
        x1t = [sb(f"x1t{i}", [128, D], F32) for i in range(2)]
        x2t = [sb(f"x2t{i}", [128, D], F32) for i in range(2)]
        ott = [sb(f"ott{i}", [128, D], F32) for i in range(2)]
        tbf = [sb(f"tbf{i}", [128, D], BF16) for i in range(2)]
        xc = [sb(f"xc{i}", [128, D], BF16) for i in range(2)]
        xnf = [sb(f"xnf{i}", [128, D], BF16) for i in range(2)]
        wide = [sb(f"wide{i}", [128, 4096], BF16) for i in range(4)]
        sA_stats = [sb(f"sAst{i}", [128, 12], F32) for i in range(2)]
        sA_mv = [sb(f"sAmv{i}", [128, 2], F32) for i in range(2)]
        sA_rstd = [sb(f"sArs{i}", [128, 1], F32) for i in range(2)]
        sA_nm = [sb(f"sAnm{i}", [128, 1], F32) for i in range(2)]
        sA_nmb = [sb(f"sAnb{i}", [128, 1], BF16) for i in range(2)]
        sA_diag = [sb(f"sAdg{i}", [128, 128], BF16) for i in range(2)]
        rk2 = [sb(f"rk2{i}", [2, 128], BF16) for i in range(2)]
        PT = [sb(f"PT{i}", [128, 256], BF16) for i in range(8)]
        stats = [sb(f"stats{i}", [128, 12], F32) for i in range(2)]
        mv = [sb(f"mv{i}", [128, 2], F32) for i in range(2)]
        negm = [sb(f"negm{i}", [128, 1], F32) for i in range(2)]
        nmb = [sb(f"nmb{i}", [128, 1], BF16) for i in range(2)]
        nmrow = [sb(f"nmrow{i}", [1, 512], BF16) for i in range(2)]
        rstd = [sb(f"rstd{i}", [128, 1], F32) for i in range(2)]
        diag = [sb(f"diag{i}", [128, 128], BF16) for i in range(2)]
        ident = sb("identb", [128, 128], BF16)
        orow = sb("orowb", [1, 128], BF16)
        morow = sb("morowb", [1, 128], BF16)
        ocol = sb("ocolb", [128, 1], BF16)
        bpj_f = sb("bpjf", [1, D], F32)
        bfn_f = sb("bfnf", [1, D], F32)
        bpj_b = sb("bpjb", [1, D], BF16)
        bfn_b = sb("bfnb", [2, D], BF16)
        sAT = [xt[1][0:32, 0:128], xt[1][0:32, 128:256]]
        sBT = [xt[1][0:16, 256:320], xt[1][0:16, 320:384]]
        sU = xt[0][0:32, 0:512]
        sT = xt[0][0:16, 512:1024]
        sums = sb("sums", [128, 2 * NQG], F32)
        rsum = sb("rsum", [128, 2], F32)
        epsb = sb("epsb", [128, 1], F32)
        xcA = [wstage[:, 1536:2048].bitcast(BF16), wstage[:, 2048:2560].bitcast(BF16)]
        nmrowA = [wstage[0:1, 2560:2816].bitcast(BF16),
                  wstage[0:1, 2816:3072].bitcast(BF16)]

        pb = [ctx.enter_context(nc.psum_tensor(f"pb{i}", [128, 512], F32))
              for i in range(8)]

        K = Kern(nc, ctx)
        block = ctx.enter_context(nc.Block())

        def WS(c0, c1):
            return [f"ws{i}" for i in range(c0 // 512, (c1 + 511) // 512)]

        def ld(queue, reads, writes, dst, src, transpose=False):
            if transpose:
                K.dma(queue, reads, writes,
                      lambda e, d=dst, s=src: e.dma_start_transpose(d, s))
            else:
                K.dma(queue, reads, writes,
                      lambda e, d=dst, s=src: e.dma_start(out=d, in_=s))

        K.op("ve", [], ["epsb"], lambda e: e.memset(epsb[:], float(EPS)))
        ld("sp", [], ["ident"], ident[:], id_in)
        ld("sp", [], ["orow"], orow[:], orow_in)
        ld("sp", [], ["morow"], morow[:], mrow_in)
        ld("sp", [], ["ocol"], ocol[:], ocol_in)
        ld("sp", [], ["bpj_f"], bpj_f[:], bpj_in)
        ld("sp", [], ["bfn_f"], bfn_f[:], bfn_in)
        K.op("act", ["bpj_f"], ["bpj_b"],
             lambda e: e.copy(bpj_b[:], bpj_f[:]))
        K.op("act", ["bfn_f"], ["bfn_b"],
             lambda e: e.mul(bfn_b[0:1, :], bfn_f[:], 0.5))

        ld("sp", [], ["xt1"], sAT[0], atR_in)
        ld("sp", [], ["xt1"], sAT[1], atI_in)
        ld("sp", [], ["xt1"], sBT[0], btR_in)
        ld("sp", [], ["xt1"], sBT[1], btI_in)
        ld("sp", [], ["xt0"], sU, U_in)
        ld("sp", [], ["xt0"], sT, T_in)


        def wv(ri, a, b):
            return wide[2 + ri][:, 2 * a:2 * b].bitcast(F32)

        for ch in range(4):
            for ri in range(2):
                atn, btn = "xt1", "xt1"
                pY, pZ = pb[2 * ri], pb[2 * ri + 1]
                pYn, pZn = f"pb{2 * ri}", f"pb{2 * ri + 1}"
                for c in range(4):
                    K.op("pe", ["xt0", atn], [pYn],
                         lambda e, c=c, ch=ch, ri=ri, pY=pY:
                         e.matmul(pY[:, c * 32:(c + 1) * 32],
                                  sU[:, ch * 128:(ch + 1) * 128],
                                  sAT[ri][:, c * 32:(c + 1) * 32],
                                  start=True, stop=True))
                    K.op("pe", ["xt0", btn], [pZn],
                         lambda e, c=c, ch=ch, ri=ri, pZ=pZ:
                         e.matmul(pZ[:, c * 16:(c + 1) * 16],
                                  sT[:, ch * 128:(ch + 1) * 128],
                                  sBT[ri][:, c * 16:(c + 1) * 16],
                                  start=True, stop=True))
                wsn_l = [f"wide{2 + ri}"]
                K.op("ve", [pYn], wsn_l,
                     lambda e, ri=ri, pY=pY:
                     e.tensor_copy(wv(ri, 0, 128), pY[:, 0:128]))
                K.op("ve", [pZn], wsn_l,
                     lambda e, ri=ri, pZ=pZ:
                     e.tensor_copy(wv(ri, 128, 192), pZ[:, 0:64]))
                for c in range(4):
                    K.op("ve", wsn_l, wsn_l,
                         lambda e, ri=ri, c=c:
                         e.tensor_tensor(
                             (wv(ri, 256, 768) if c == 0 else wv(ri, 768, 1280))
                             .rearrange("p (a b) -> p a b", a=32),
                             wv(ri, c * 32, (c + 1) * 32)
                             .unsqueeze(2).broadcast_to([128, 32, 16]),
                             wv(ri, 128 + c * 16, 128 + (c + 1) * 16)
                             .unsqueeze(1).broadcast_to([128, 32, 16]),
                             ALU.mult))
                    if c > 0:
                        K.op("ve", wsn_l, wsn_l,
                             lambda e, ri=ri:
                             e.tensor_add(wv(ri, 256, 768),
                                          wv(ri, 256, 768),
                                          wv(ri, 768, 1280)))
                if ri == 0:
                    K.op("act", wsn_l, ["Wc"],
                         lambda e, ch=ch, ri=ri:
                         e.mul(Wc[:, ch * D + 0:ch * D + 512],
                               wv(ri, 256, 768), 0.5))
                    K.op("act", wsn_l, ["Wc"],
                         lambda e, ch=ch, ri=ri:
                         e.mul(Wc[:, (4 + ch) * D + 512:(4 + ch) * D + 1024],
                               wv(ri, 256, 768), 0.5))
                else:
                    K.op("act", wsn_l, ["Wc"],
                         lambda e, ch=ch, ri=ri:
                         e.mul(Wc[:, ch * D + 512:ch * D + 1024],
                               wv(ri, 256, 768), 0.5))
                    K.op("act", wsn_l, ["Wc"],
                         lambda e, ch=ch, ri=ri:
                         e.mul(Wc[:, (4 + ch) * D + 0:(4 + ch) * D + 512],
                               wv(ri, 256, 768), -0.5))
                if ri == 1:
                    for fc in (ch, 4 + ch):
                        for half_ in range(2):
                            K.op("pe", ["Wc", "ocol"], [f"pb{4 + half_}"],
                                 lambda e, fc=fc, half_=half_, ch=ch:
                                 e.matmul(pb[4 + half_][0:1, 0:512], ocol[:],
                                          Wc[:, fc * D + half_ * 512:
                                             fc * D + half_ * 512 + 512],
                                          start=(ch == 0 and fc == ch),
                                          stop=(ch == 3 and fc == 4 + ch)))

        for half_ in range(2):
            K.op("act", [f"pb{4 + half_}"], ["ws5"],
                 lambda e, half_=half_:
                 e.mul(nmrowA[0][0:1, 0:512], pb[4 + half_][0:1, 0:512], -1.0))
            K.dma("gp", ["ws5"], ["bfn_b"],
                  lambda e, half_=half_:
                  e.dma_start(out=bfn_b[1:2, half_ * 512:half_ * 512 + 512],
                              in_=nmrowA[0][0:1, 0:512]))
        for par in range(2):
            K.op("ve", [], [f"rk2{par}"],
                 lambda e, par=par: e.memset(rk2[par][0:1, :], 1.0))

        def layernormA(par, src_t, src_n):
            sn, mn, rn = f"sAst{par}", f"sAmv{par}", f"sArs{par}"
            xn_ = "ws3" if par == 0 else "ws4"
            K.op("ve", [src_n], [sn],
                 lambda e: e.bn_stats(sA_stats[par][:, 0:6], src_t[:, 0:512]))
            K.op("ve", [src_n], [sn],
                 lambda e: e.bn_stats(sA_stats[par][:, 6:12], src_t[:, 512:1024]))
            K.op("ve", [sn], [mn],
                 lambda e: e.bn_aggr(sA_mv[par][:], sA_stats[par][:]))
            K.op("act", [mn, "epsb"], [rn],
                 lambda e: e.activation(sA_rstd[par][:], sA_mv[par][:, 1:2],
                                        AF.Sqrt, bias=epsb[:]))
            K.op("ve", [rn], [rn],
                 lambda e: e.reciprocal(sA_rstd[par][:], sA_rstd[par][:]))
            K.op("act", [src_n], [xn_],
                 lambda e: e.copy(xcA[par][:], src_t[:]))
            K.op("ve", ["ident", rn], [f"sAdg{par}"],
                 lambda e: e.tensor_scalar_mul(sA_diag[par][:], ident[:],
                                               sA_rstd[par][:]))
            K.op("ve", [mn, rn], [f"sAnb{par}"],
                 lambda e: e.tensor_mul(sA_nmb[par][:], sA_mv[par][:, 0:1],
                                        sA_rstd[par][:]))
            prow = pb[7][0:1, 0:64].bitcast(BF16)
            K.op("pe", [f"sAnb{par}", "ident"], ["pb7"],
                 lambda e, prow=prow:
                 e.transpose(prow, sA_nmb[par][:, 0:1], ident[:]))
            K.op("ve", ["pb7"], ["ws5"],
                 lambda e, prow=prow:
                 e.tensor_copy(nmrowA[par][0:1, 0:128], prow))
            K.dma("gp", ["ws5"], [f"rk2{par}"],
                  lambda e: e.dma_start(out=rk2[par][1:2, :],
                                        in_=nmrowA[par][0:1, 0:128]))

        def ln_transposeA(par, pbank_a, pbank_b, dst, dstA, dstB):
            pa, pan = pb[pbank_a], f"pb{pbank_a}"
            pc, pcn = pb[pbank_b], f"pb{pbank_b}"
            xn_ = "ws3" if par == 0 else "ws4"
            for fc in range(4):
                K.op("pe", [xn_, f"sAdg{par}"], [pan],
                     lambda e, fc=fc:
                     e.matmul(pa[:, (fc % 4) * 128:(fc % 4) * 128 + 128],
                              xcA[par][:, fc * 128:(fc + 1) * 128],
                              sA_diag[par][:], start=(fc % 4 == 0),
                              stop=(fc % 4 == 3), skip_group_check=True))
            K.op("act", [pan], [dstA],
                 lambda e: e.copy(dst[:, 0:512], pa[:]))
            for fc in range(4, 8):
                K.op("pe", [xn_, f"sAdg{par}"], [pcn],
                     lambda e, fc=fc:
                     e.matmul(pc[:, (fc % 4) * 128:(fc % 4) * 128 + 128],
                              xcA[par][:, fc * 128:(fc + 1) * 128],
                              sA_diag[par][:], start=(fc % 4 == 0),
                              stop=(fc % 4 == 3), skip_group_check=True))
            K.op("act", [pcn], [dstB],
                 lambda e: e.copy(dst[:, 512:1024], pc[:]))

        def layernorm(par, src_t, src_n):
            sn, mn, nn_, rn, dn = (f"stats{par}", f"mv{par}", f"negm{par}",
                                   f"rstd{par}", f"diag{par}")
            K.op("ve", [src_n], [sn],
                 lambda e: e.bn_stats(stats[par][:, 0:6], src_t[:, 0:512]))
            K.op("ve", [src_n], [sn],
                 lambda e: e.bn_stats(stats[par][:, 6:12], src_t[:, 512:1024]))
            K.op("ve", [sn], [mn],
                 lambda e: e.bn_aggr(mv[par][:], stats[par][:]))
            K.op("act", [mn, "epsb"], [rn],
                 lambda e: e.activation(rstd[par][:], mv[par][:, 1:2], AF.Sqrt,
                                        bias=epsb[:]))
            K.op("ve", [rn], [rn],
                 lambda e: e.reciprocal(rstd[par][:], rstd[par][:]))
            K.op("act", [src_n], [f"xc{par}"],
                 lambda e: e.copy(xc[par][:], src_t[:]))
            K.op("ve", ["ident", rn], [dn],
                 lambda e: e.tensor_scalar_mul(diag[par][:], ident[:],
                                               rstd[par][:]))
            K.op("ve", [mn, rn], [f"nmb{par}"],
                 lambda e: e.tensor_mul(nmb[par][:], mv[par][:, 0:1],
                                        rstd[par][:]))
            prow = pb[7][0:1, 0:64].bitcast(BF16)
            K.op("pe", [f"nmb{par}", "ident"], ["pb7"],
                 lambda e, prow=prow:
                 e.transpose(prow, nmb[par][:, 0:1], ident[:]))
            K.op("ve", ["pb7"], [f"nmrow{par}"],
                 lambda e, prow=prow:
                 e.tensor_copy(
                     nmrow[par][0:1, :].rearrange("o (f t) -> o f t", f=4),
                     prow.unsqueeze(1).broadcast_to([1, 4, 128])))

        def ln_transpose(par, pbank_a, pbank_b, dst, dstA, dstB):
            pa, pan = pb[pbank_a], f"pb{pbank_a}"
            pc, pcn = pb[pbank_b], f"pb{pbank_b}"
            for fc in range(4):
                K.op("pe", [f"xc{par}", f"diag{par}"], [pan],
                     lambda e, fc=fc:
                     e.matmul(pa[:, (fc % 4) * 128:(fc % 4) * 128 + 128],
                              xc[par][:, fc * 128:(fc + 1) * 128],
                              diag[par][:], start=(fc % 4 == 0), stop=False,
                              skip_group_check=True))
            K.op("pe", ["morow", f"nmrow{par}"], [pan],
                 lambda e:
                 e.matmul(pa[:], morow[:], nmrow[par][0:1, :],
                          start=False, stop=True, skip_group_check=True))
            K.op("act", [pan], [dstA],
                 lambda e: e.copy(dst[:, 0:512], pa[:]))
            for fc in range(4, 8):
                K.op("pe", [f"xc{par}", f"diag{par}"], [pcn],
                     lambda e, fc=fc:
                     e.matmul(pc[:, (fc % 4) * 128:(fc % 4) * 128 + 128],
                              xc[par][:, fc * 128:(fc + 1) * 128],
                              diag[par][:], start=(fc % 4 == 0), stop=False,
                              skip_group_check=True))
            K.op("pe", ["morow", f"nmrow{par}"], [pcn],
                 lambda e:
                 e.matmul(pc[:], morow[:], nmrow[par][0:1, :],
                          start=False, stop=True, skip_group_check=True))
            K.op("act", [pcn], [dstB],
                 lambda e: e.copy(dst[:, 512:1024], pc[:]))

        def ffn(par, pbank0, pbank1):
            p0, p0n = pb[pbank0], f"pb{pbank0}"
            p1, p1n = pb[pbank1], f"pb{pbank1}"
            for half_, (p, pn) in enumerate(((p0, p0n), (p1, p1n))):
                for fc in range(8):
                    xr = f"xnfA{par}" if fc < 4 else f"xnfB{par}"
                    K.op("pe", [xr, "Wc"], [pn],
                         lambda e, fc=fc, p=p, half_=half_:
                         e.matmul(p[:], xnf[par][:, fc * 128:(fc + 1) * 128],
                                  Wc[:, fc * D + half_ * 512:fc * D + half_ * 512 + 512],
                                  start=(fc == 0), stop=False))
                K.op("pe", [f"rk2{par}", "bfn_b"], [pn],
                     lambda e, p=p, half_=half_:
                     e.matmul(p[:], rk2[par][0:2, :],
                              bfn_b[0:2, half_ * 512:half_ * 512 + 512],
                              start=False, stop=True))

        def p1b_load(s, g):
            par = g % 2
            ld("gp", [f"xn2d{s}t{4 * g + i}" for i in range(4)], [f"wide{par}"],
               wide[par][:].rearrange("p (fc t) -> p fc t", fc=8),
               xn2d[:, :, s * S + g * 512:s * S + (g + 1) * 512]
               .rearrange("fc p t -> p fc t"))

        def pass1a(s, first=False):
            def load(t):
                par = t % 2
                rows = slice(s * S + t * 128, s * S + (t + 1) * 128)
                ld("sp", [], [f"xt{par}"], xt[par][:], x_in[rows, :])

            def compA1(t):
                par = t % 2
                ln_transposeA(par, 4 + par, 6, xnf[par], f"xnfA{par}", f"xnfB{par}")
                ffn(par, 2 * par, 2 * par + 1)

            def compA2(t):
                par = t % 2
                rows = slice(s * S + t * 128, s * S + (t + 1) * 128)
                xn, x1n = f"xt{par}", f"x1t{par}"
                p0, p0n = pb[2 * par], f"pb{2 * par}"
                p1, p1n = pb[2 * par + 1], f"pb{2 * par + 1}"
                K.op("ve", [p0n, xn], [x1n],
                     lambda e, par=par, p0=p0:
                     e.tensor_add(x1t[par][:, 0:512], p0[:], xt[par][:, 0:512]))
                K.op("ve", [p1n, xn], [x1n],
                     lambda e, par=par, p1=p1:
                     e.tensor_add(x1t[par][:, 512:1024], p1[:], xt[par][:, 512:1024]))
                K.dma("sp", [x1n], [f"x1d{s}"],
                      lambda e, par=par, rows=rows:
                      e.dma_start(out=x1d[rows, :], in_=x1t[par][:]))

            def chainB(t):
                par = t % 2
                layernorm(par, x1t[par], f"x1t{par}")

            def compB2(t):
                par = t % 2
                ln_transpose(par, 4 + par, 6, tbf[par], f"tbfA{par}", f"tbfB{par}")
                K.dma("gp", [f"tbfA{par}", f"tbfB{par}"], [f"xn2d{s}t{t}"],
                      lambda e, par=par, s=s, t=t:
                      e.dma_start(
                          out=xn2d[:, :, s * S + t * 128:s * S + (t + 1) * 128]
                          .rearrange("fc p t -> p fc t"),
                          in_=tbf[par][:].rearrange("p (fc t) -> p fc t", fc=8)))

            load(0)
            load(1)
            layernormA(0, xt[0], "xt0")
            layernormA(1, xt[1], "xt1")
            compA1(0)
            for t in range(NT):
                par = t % 2
                if t + 1 < NT:
                    compA1(t + 1)
                compA2(t)
                chainB(t)
                if t + 2 < NT:
                    load(t + 2)
                if t < 16:
                    ec, r0 = t // 2, (t % 2) * 64
                    K.dma("gp", [], ["bbK0a", "bbK0", "bbK1", "bbV0", "bbV1"],
                          lambda e, ec=ec, r0=r0:
                          e.dma_start(
                              out=bigbuf[r0:r0 + 64, ec * 3072:(ec + 1) * 3072],
                              in_=wqT_in[ec * 128 + r0:ec * 128 + r0 + 64, :]))
                if t in (6, 8):
                    p1b_load(s, 0 if t == 6 else 1)
                t0w = 16 if NT >= 24 else 8
                if first and t0w <= t < t0w + 8:
                    kc = t - t0w
                    K.dma("gp", [], ["wpT"],
                          lambda e, kc=kc:
                          e.dma_start(out=wpT[:, kc * 1024:(kc + 1) * 1024],
                                      in_=wpT_in[kc * 128:(kc + 1) * 128, :]))
                compB2(t)
                if t + 2 < NT:
                    layernormA(par, xt[par], f"xt{par}")

        def pass1b(s):
            bball = ["bbK0a", "bbK0", "bbK1", "bbV0", "bbV1"]

            def wq(ec, h0, w):
                return bigbuf[:, ec * 3072 + h0: ec * 3072 + h0 + w]

            def load(g):
                p1b_load(s, g)

            def comp(g):
                par = g % 2
                xfn = f"wide{par}"
                xf = wide[par]
                for qk, dram, dname in ((0, qd, "qd"), (1, kd, "kd")):
                    stg = wide[2 + qk]
                    stgn = f"wide{2 + qk}"
                    for hc in range(8):
                        pB, pBn = pb[hc % 2], f"pb{hc % 2}"
                        for ec in range(8):
                            K.op("pe", bball + [xfn], [pBn],
                                 lambda e, pB=pB, ec=ec, hc=hc, qk=qk, xf=xf:
                                 e.matmul(pB[:],
                                          wq(ec, qk * 1024 + hc * 128, 128),
                                          xf[:, ec * 512:(ec + 1) * 512],
                                          start=(ec == 0), stop=(ec == 7)))
                        if qk == 0:
                            K.op("act", [pBn], [stgn],
                                 lambda e, pB=pB, stg=stg, hc=hc:
                                 e.copy(stg[:, hc * 512:(hc + 1) * 512], pB[:]))
                        else:
                            K.op("ve", [pBn], [stgn],
                                 lambda e, pB=pB, stg=stg, hc=hc:
                                 e.tensor_copy(stg[:, hc * 512:(hc + 1) * 512], pB[:]))
                    K.dma("sp", [stgn], [f"{dname}{s}g{g}"],
                          lambda e, dram=dram, stg=stg, s=s, g=g:
                          e.dma_start(
                              out=dram[s, :, :, g * 512:(g + 1) * 512]
                              .rearrange("dc p q -> p dc q"),
                              in_=stg[:].rearrange("p (dc q) -> p dc q", dc=8)))
                for ts in range(4):
                    pV0, pV0n = pb[4 + ts % 2], f"pb{4 + ts % 2}"
                    pV1, pV1n = pb[6 + ts % 2], f"pb{6 + ts % 2}"
                    for ec in range(8):
                        K.op("pe", bball + [xfn], [pV0n],
                             lambda e, pV0=pV0, ec=ec, ts=ts, xf=xf:
                             e.matmul(pV0[:],
                                      xf[:, ec * 512 + ts * 128: ec * 512 + (ts + 1) * 128],
                                      wq(ec, 2048, 512),
                                      start=(ec == 0), stop=(ec == 7)))
                        K.op("pe", bball + [xfn], [pV1n],
                             lambda e, pV1=pV1, ec=ec, ts=ts, xf=xf:
                             e.matmul(pV1[:],
                                      xf[:, ec * 512 + ts * 128: ec * 512 + (ts + 1) * 128],
                                      wq(ec, 2560, 512),
                                      start=(ec == 0), stop=(ec == 7)))
                    qsn = f"xnfA{ts % 2}"
                    K.op("ve", [pV0n], [qsn],
                         lambda e, pV0=pV0, ts=ts:
                         e.tensor_copy(xnf[ts % 2][:, 0:512], pV0[:]))
                    K.dma("sp", [qsn], [f"vd{s}g{g}"],
                          lambda e, s=s, g=g, ts=ts:
                          e.dma_start(out=vd[s, g * 4 + ts, :, 0:512],
                                      in_=xnf[ts % 2][:, 0:512]))
                    K.op("act", [pV1n], [f"tbfA{ts % 2}"],
                         lambda e, pV1=pV1, ts=ts: e.copy(tbf[ts % 2][:, 0:512], pV1[:]))
                    K.dma("sp", [f"tbfA{ts % 2}"], [f"vd{s}g{g}"],
                          lambda e, s=s, g=g, ts=ts:
                          e.dma_start(out=vd[s, g * 4 + ts, :, 512:1024],
                                      in_=tbf[ts % 2][:, 0:512]))

            for g in range(NG):
                if 2 <= g + 1 < NG:
                    load(g + 1)
                comp(g)

        def pass2(s):
            QS = NKTH // 2

            def load_kv(half):
                kb = bigbuf[:, 0:8 * (S // 2)].rearrange("p (dc k) -> p dc k",
                                                          dc=8)
                for i in range(2):
                    k0 = half * (S // 2) + i * (S // 4)
                    gl = [f"kd{s}g{g}" for g in range(k0 // 512, (k0 + S // 4) // 512)]
                    if i == 0:
                        ld("sp", gl, ["bbK0a"], kb[:, :, 0:256],
                           kd[s, :, :, k0:k0 + 256].rearrange("dc p k -> p dc k"))
                        ld("sp", gl, ["bbK0"], kb[:, :, 256:S // 4],
                           kd[s, :, :, k0 + 256:k0 + S // 4]
                           .rearrange("dc p k -> p dc k"))
                    else:
                        ld("sp", gl, [f"bbK{i}"],
                           kb[:, :, i * (S // 4):(i + 1) * (S // 4)],
                           kd[s, :, :, k0:k0 + S // 4].rearrange("dc p k -> p dc k"))
                    gl = [f"vd{s}g{g}" for g in range(k0 // 512, (k0 + S // 4) // 512)]
                    tv0 = half * VH + i * (VH // 2)
                    ld("sp", gl, [f"bbV{i}"],
                       bigbuf[:, 16384 + i * (VH // 2) * 1024:
                              16384 + (i + 1) * (VH // 2) * 1024]
                       .rearrange("p (tv h) -> p tv h", tv=VH // 2),
                       vd[s, tv0:tv0 + VH // 2, :, :].rearrange("tv p h -> p tv h"))

            def load_q(half, qg):
                par = qg % 2
                ld("sp", [f"qd{s}g{qg // 2}"], [f"wide{2 + par}"],
                   wide[2 + par][:, 0:2048].rearrange("p (dc q) -> p dc q", dc=8),
                   qd[s, :, :, qg * 256:(qg + 1) * 256]
                   .rearrange("dc p q -> p dc q"))

            def kh(dc, kt):
                o = dc * (S // 2) + kt * 128
                return bigbuf[:, o:o + 128]

            def vh(tv, h0):
                o = 16384 + tv * 1024 + h0
                return bigbuf[:, o:o + 512]

            def qgroup(half, qg):
                par = qg % 2
                qtn = f"wide{2 + par}"
                qt = wide[2 + par]

                def scores(kt):
                    pS, pSn = pb[kt % 2], f"pb{kt % 2}"
                    ptn = f"PT{kt % 8}"
                    pt = PT[kt % 8]
                    for dc in range(8):
                        K.op("pe", ["bbK0a" if kt < 2 else f"bbK{kt // QS}",
                                    qtn], [pSn],
                             lambda e, pS=pS, dc=dc, kt=kt, qt=qt:
                             e.matmul(pS[:, 0:256], kh(dc, kt),
                                      qt[:, dc * 256:(dc + 1) * 256],
                                      start=(dc == 0), stop=(dc == 7)))
                    K.op("act", [pSn], [ptn],
                         lambda e, pS=pS, pt=pt:
                         e.activation(pt[:], pS[:, 0:256], AF.Exp, scale=0.125))

                def pv(kt):
                    ptn = f"PT{kt % 8}"
                    pt = PT[kt % 8]
                    for qc in range(2):
                        for hf in range(2):
                            pO, pOn = pb[2 + 2 * qc + hf], f"pb{2 + 2 * qc + hf}"
                            K.op("pe", [ptn, f"bbV{kt // QS}"], [pOn],
                                 lambda e, pO=pO, pt=pt, qc=qc, hf=hf, kt=kt:
                                 e.matmul(pO[:], pt[:, qc * 128:(qc + 1) * 128],
                                          vh(kt, hf * 512),
                                          start=(kt == 0), stop=(kt == NKTH - 1)))
                        K.op("pe", [ptn, "ocol"], [f"pb{6 + qc}"],
                             lambda e, pt=pt, qc=qc, kt=kt:
                             e.matmul(pb[6 + qc][:, 0:1],
                                      pt[:, qc * 128:(qc + 1) * 128], ocol[:],
                                      start=(kt == 0), stop=(kt == NKTH - 1)))

                for kt in range(NKTH):
                    scores(kt)
                    if kt >= 1:
                        pv(kt - 1)
                pv(NKTH - 1)
                if half == 0:
                    for qc in range(2):
                        K.op("ve", [f"pb{6 + qc}"], ["sums"],
                             lambda e, qg=qg, qc=qc:
                             e.tensor_copy(sums[:, 2 * qg + qc:2 * qg + qc + 1],
                                           pb[6 + qc][:, 0:1]))
                    for qc in range(2):
                        for hf in range(2):
                            o0 = qc * 1024 + hf * 512
                            if hf == 0:
                                K.op("ve", [f"pb{2 + 2 * qc + hf}"],
                                     WS(o0, o0 + 512),
                                     lambda e, qc=qc, hf=hf, o0=o0:
                                     e.tensor_copy(wstage[:, o0:o0 + 512],
                                                   pb[2 + 2 * qc + hf][:]))
                            else:
                                K.op("act", [f"pb{2 + 2 * qc + hf}"],
                                     WS(o0, o0 + 512),
                                     lambda e, qc=qc, hf=hf, o0=o0:
                                     e.copy(wstage[:, o0:o0 + 512],
                                            pb[2 + 2 * qc + hf][:]))
                        K.dma("gp", WS(qc * 1024, (qc + 1) * 1024), ["opd"],
                              lambda e, qc=qc, qg=qg:
                              e.dma_start(
                                  out=opd[qg * 256 + qc * 128:
                                          qg * 256 + (qc + 1) * 128, :],
                                  in_=wstage[:, qc * 1024:(qc + 1) * 1024]))
                else:
                    for qc in range(2):
                        K.op("ve", [f"pb{6 + qc}", "sums"], ["rsum"],
                             lambda e, qg=qg, qc=qc:
                             e.tensor_add(rsum[:, qc:qc + 1], pb[6 + qc][:, 0:1],
                                          sums[:, 2 * qg + qc:2 * qg + qc + 1]))
                    K.op("ve", ["rsum"], ["rsum"],
                         lambda e: e.reciprocal(rsum[:], rsum[:]))
                    for qc in range(2):
                        ld("sp", ["opd"], WS(qc * 1024, (qc + 1) * 1024),
                           wstage[:, qc * 1024:(qc + 1) * 1024],
                           opd[qg * 256 + qc * 128:qg * 256 + (qc + 1) * 128, :])
                        for hf in range(2):
                            o0 = qc * 1024 + hf * 512
                            K.op("ve",
                                 [f"pb{2 + 2 * qc + hf}"] + WS(o0, o0 + 512),
                                 WS(o0, o0 + 512),
                                 lambda e, qc=qc, hf=hf, o0=o0:
                                 e.tensor_add(
                                     wstage[:, o0:o0 + 512],
                                     pb[2 + 2 * qc + hf][:],
                                     wstage[:, o0:o0 + 512]))
                        K.op("ve", WS(qc * 1024, (qc + 1) * 1024) + ["rsum"],
                             [f"xnfA{qc}", f"xnfB{qc}"],
                             lambda e, qc=qc:
                             e.tensor_tensor(
                                 xnf[qc][:],
                                 wstage[:, qc * 1024:(qc + 1) * 1024],
                                 rsum[:, qc:qc + 1].broadcast_to([128, 1024]),
                                 ALU.mult))
                        K.dma("gp", [f"xnfA{qc}", f"xnfB{qc}"],
                              [f"od{s}q{qg}"],
                              lambda e, qc=qc, qg=qg, s=s:
                              e.dma_start(
                                  out=od[s * S + qg * 256 + qc * 128:
                                         s * S + qg * 256 + (qc + 1) * 128, :],
                                  in_=xnf[qc][:]))

            for half in range(2):
                load_q(half, 0)
                load_kv(half)
                for qg in range(NQG):
                    if qg + 1 < NQG:
                        load_q(half, qg + 1)
                    qgroup(half, qg)

        def pass3(s):
            x1v = x1d.rearrange("(sq h f) e -> sq h f e", sq=n_seq, f=CF)
            outv = out_d.rearrange("(sq h f) e -> sq h f e", sq=n_seq, f=CF)

            def load(mt):
                c, hc = mt // 8, mt % 8
                par = mt % 2
                ld("sp", [f"od{s}q{q}" for q in range(4 * c, 4 * c + 4)],
                   [f"tbfA{par}", f"tbfB{par}"],
                   tbf[par][:].rearrange("p (kc w) -> p kc w", kc=8),
                   od[s * S + c * 1024:s * S + (c + 1) * 1024,
                      hc * 128:(hc + 1) * 128]
                   .rearrange("(kc p) w -> p kc w", kc=8))
                ld("sp", [f"x1d{s}"], [f"x1t{par}"], x1t[par][:],
                   x1v[s, hc * 128:(hc + 1) * 128, c, :])

            def compA(mt):
                c, hc = mt // 8, mt % 8
                par = mt % 2
                ob = tbf[par]
                obn = f"tbfA{par}"
                p0, p0n = pb[2 * par], f"pb{2 * par}"
                p1, p1n = pb[2 * par + 1], f"pb{2 * par + 1}"
                for half_, (p, pn) in enumerate(((p0, p0n), (p1, p1n))):
                    for kc in range(8):
                        K.op("pe", [obn, f"tbfB{par}", "wpT"], [pn],
                             lambda e, p=p, kc=kc, ob=ob, half_=half_:
                             e.matmul(p[:], ob[:, kc * 128:(kc + 1) * 128],
                                      wpT[:, kc * D + half_ * 512:
                                          kc * D + half_ * 512 + 512],
                                      start=(kc == 0), stop=False))
                    K.op("pe", ["orow", "bpj_b"], [pn],
                         lambda e, p=p, half_=half_:
                         e.matmul(p[:], orow[:],
                                  bpj_b[0:1, half_ * 512:half_ * 512 + 512],
                                  start=False, stop=True))
            def compA2(mt):
                par = mt % 2
                p0, p0n = pb[2 * par], f"pb{2 * par}"
                p1, p1n = pb[2 * par + 1], f"pb{2 * par + 1}"
                K.op("ve", [p0n, f"x1t{par}"], [f"x2t{par}"],
                     lambda e, par=par, p0=p0:
                     e.tensor_add(x2t[par][:, 0:512], p0[:], x1t[par][:, 0:512]))
                K.op("ve", [p1n, f"x1t{par}"], [f"x2t{par}"],
                     lambda e, par=par, p1=p1:
                     e.tensor_add(x2t[par][:, 512:1024], p1[:],
                                  x1t[par][:, 512:1024]))

            def compB(mt):
                c, hc = mt // 8, mt % 8
                par = mt % 2
                ln_transposeA(par, 4 + par, 6, xnf[par], f"xnfA{par}", f"xnfB{par}")
                ffn(par, 2 * par, 2 * par + 1)
                K.op("ve", [f"pb{2 * par}", f"x2t{par}"], [f"ott{par}"],
                     lambda e, par=par:
                     e.tensor_add(ott[par][:, 0:512], pb[2 * par][:],
                                  x2t[par][:, 0:512]))
                K.op("ve", [f"pb{2 * par + 1}", f"x2t{par}"], [f"ott{par}"],
                     lambda e, par=par:
                     e.tensor_add(ott[par][:, 512:1024], pb[2 * par + 1][:],
                                  x2t[par][:, 512:1024]))
                K.dma("gp", [f"ott{par}"], ["outd"],
                      lambda e, par=par, s=s, c=c, hc=hc:
                      e.dma_start(out=outv[s, hc * 128:(hc + 1) * 128, c, :],
                                  in_=ott[par][:]))

            load(0)
            load(1)
            compA(0)
            for mt in range(NMT):
                par = mt % 2
                if mt + 1 < NMT:
                    compA(mt + 1)
                compA2(mt)
                layernormA(par, x2t[par], f"x2t{par}")
                if mt + 2 < NMT:
                    load(mt + 2)
                compB(mt)

        for s in range(n_seq):
            pass1a(s, first=(s == 0))
            pass1b(s)
            pass2(s)
            pass3(s)

        final_counts = [(f"gp{i}", K.cnt[f"gp{i}"]) for i in range(4)
                        if K.cnt[f"gp{i}"] > 0]

        def fin(e, fc=final_counts, sems=K.sems):
            for s, v in fc:
                e.wait_ge(sems[s], v)
            return e.engine_nop()

        K.op("po", [], [], fin)

        K.finalize(block)

    return nc


def make_weight_inputs(w_qkv, w_proj, b_proj, G1_R, G2_R, G1_I, G2_I,
                       bias_real, bias_imag):
    f32 = np.float32
    bf16 = ml_dtypes.bfloat16
    U = np.zeros((32, 512), dtype=f32)
    T = np.zeros((16, 512), dtype=f32)
    for p in range(512):
        U[p // 16, p] = 1.0
        T[p % 16, p] = 1.0
    im = {
        "wqT": np.ascontiguousarray(np.asarray(w_qkv, dtype=f32).T),
        "wpT": np.ascontiguousarray(np.asarray(w_proj, dtype=f32).T),
        "bpj": np.asarray(b_proj, dtype=f32).reshape(1, 1024),
        "bfn": np.concatenate([np.asarray(bias_real, dtype=f32),
                               np.asarray(bias_imag, dtype=f32)]).reshape(1, 1024),
        "atR": np.ascontiguousarray(np.transpose(
            np.asarray(G1_R, dtype=f32)[0], (2, 1, 0)).reshape(32, 128)),
        "atI": np.ascontiguousarray(np.transpose(
            np.asarray(G1_I, dtype=f32)[0], (2, 1, 0)).reshape(32, 128)),
        "btR": np.ascontiguousarray(np.transpose(
            np.asarray(G2_R, dtype=f32)[:, :, 0, :], (2, 0, 1)).reshape(16, 64)),
        "btI": np.ascontiguousarray(np.transpose(
            np.asarray(G2_I, dtype=f32)[:, :, 0, :], (2, 0, 1)).reshape(16, 64)),
        "Udel": U,
        "Tdel": T,
        "ident": np.eye(128, dtype=bf16),
        "orow": np.ones((1, 128), dtype=bf16),
        "morow": -np.ones((1, 128), dtype=bf16),
        "ocol": np.ones((128, 1), dtype=bf16),
    }
    return im


def kernel(L0_attn_x, L0_attn_w_qkv, L0_attn_w_proj, L0_attn_b_proj,
           L0_ffn_G1_R, L0_ffn_G2_R, L0_ffn_G1_I, L0_ffn_G2_I,
           L0_ffn_bias_real, L0_ffn_bias_imag):
    x = np.asarray(L0_attn_x, dtype=np.float32)
    B, S, _ = x.shape
    n_seq = B // N_CORES
    wim = make_weight_inputs(L0_attn_w_qkv, L0_attn_w_proj, L0_attn_b_proj,
                             L0_ffn_G1_R, L0_ffn_G2_R, L0_ffn_G1_I, L0_ffn_G2_I,
                             L0_ffn_bias_real, L0_ffn_bias_imag)
    in_maps = []
    for c in range(N_CORES):
        im = dict(wim)
        im["x"] = np.ascontiguousarray(
            x[c * n_seq:(c + 1) * n_seq].reshape(n_seq * S, D))
        in_maps.append(im)
    nc = build_program(n_seq=n_seq, S=S)
    res = run_bass_kernel_spmd(nc, in_maps, core_ids=list(range(N_CORES)))
    out = np.concatenate([np.asarray(r["out"]).reshape(n_seq, S, D)
                          for r in res.results], axis=0)
    return out


# version 36
# speedup vs baseline: 1.0207x; 1.0207x over previous
import numpy as np
import ml_dtypes
from contextlib import ExitStack

import concourse.bass as bass
import concourse.mybir as mybir
from concourse.bass_utils import run_bass_kernel_spmd

F32 = mybir.dt.float32
BF16 = mybir.dt.bfloat16
AF = mybir.ActivationFunctionType
ALU = mybir.AluOpType

D = 1024
EPS = 1e-5
N_CORES = 8


class Kern:
    COMPUTE = ("pe", "ve", "act", "po")

    def __init__(self, nc, ctx):
        self.nc = nc
        self.sems = {}
        self.cnt = {}
        for name in ("pe", "ve", "act", "po"):
            self.sems[name] = ctx.enter_context(nc.semaphore("S_" + name))
            self.cnt[name] = 0
        for q in ("sp", "gp"):
            for i in range(4):
                n = f"{q}{i}"
                self.sems[n] = ctx.enter_context(nc.semaphore("S_" + n))
                self.cnt[n] = 0
        self.rr = {"sp": 0, "gp": 0}
        self.stream = {e: [] for e in ("pe", "ve", "act", "po", "sp", "gp")}
        self.waited = {e: {} for e in self.stream}
        self.regions = {}

    def _engine_stream(self, eng):
        return "gp" if eng == "po" else eng

    def _enqueue(self, eng, deps, fn, inc_sem, inc_val):
        est = self._engine_stream(eng)
        need = {}
        for (s, v) in deps:
            if v > need.get(s, 0):
                need[s] = v
        waited = self.waited[est]
        waits = []
        for s, v in sorted(need.items()):
            if s == "pe" and est == "pe":
                continue
            if waited.get(s, 0) < v:
                waits.append((s, v))
                waited[s] = v
        self.stream[est].append((waits, fn, inc_sem, inc_val))
        return (inc_sem, self.cnt[inc_sem])

    def _deps_and_update(self, reads, writes, tok_fn):
        deps = []
        for r in reads:
            reg = self.regions.get(r)
            if reg:
                deps += reg["w"]
        for w in writes:
            reg = self.regions.get(w)
            if reg:
                deps += reg["r"] + reg["w"]
        tok = tok_fn(deps)
        for r in reads:
            self.regions.setdefault(r, {"w": [], "r": []})["r"].append(tok)
        for w in writes:
            self.regions[w] = {"w": [tok], "r": []}
        return tok

    def op(self, eng, reads, writes, fn):
        def go(deps):
            self.cnt[eng] += 1
            return self._enqueue(eng, deps, fn, eng, 1)
        return self._deps_and_update(reads, writes, go)

    def dma(self, queue, reads, writes, fn):
        def go(deps):
            i = self.rr[queue]
            self.rr[queue] = (i + 1) % 4
            sem = f"{queue}{i}"
            if self.cnt[sem] > 0:
                deps = deps + [(sem, self.cnt[sem])]
            self.cnt[sem] += 16
            return self._enqueue(queue, deps, fn, sem, 16)
        return self._deps_and_update(reads, writes, go)

    def finalize(self, block):
        engines = {
            "pe": block.tensor, "ve": block.vector, "act": block.scalar,
            "sp": block.sync, "gp": block.gpsimd,
        }
        for est, section in engines.items():
            stream = self.stream[est]
            sems = self.sems

            def emit(engobj, stream=stream, sems=sems):
                for waits, fn, inc_sem, inc_val in stream:
                    for s, v in waits:
                        engobj.wait_ge(sems[s], v)
                    ins = fn(engobj)
                    ins.then_inc(sems[inc_sem], inc_val)

            section(emit)


def build_program(n_seq=2, S=4096):
    CF = S // 1024
    NT = S // 128
    NG = S // 512
    NQG = S // 256
    NKTH = S // 256
    NMT = CF * 8
    VH = S // 256

    nc = bass.Bass("TRN2", target_bir_lowering=False)

    x_in = nc.dram_tensor("x", [n_seq * S, D], F32, kind="ExternalInput").ap()
    wqT_in = nc.dram_tensor("wqT", [D, 3 * D], F32, kind="ExternalInput").ap()
    wpT_in = nc.dram_tensor("wpT", [D, D], F32, kind="ExternalInput").ap()
    bpj_in = nc.dram_tensor("bpj", [1, D], F32, kind="ExternalInput").ap()
    bfn_in = nc.dram_tensor("bfn", [1, D], F32, kind="ExternalInput").ap()
    atR_in = nc.dram_tensor("atR", [32, 128], F32, kind="ExternalInput").ap()
    atI_in = nc.dram_tensor("atI", [32, 128], F32, kind="ExternalInput").ap()
    btR_in = nc.dram_tensor("btR", [16, 64], F32, kind="ExternalInput").ap()
    btI_in = nc.dram_tensor("btI", [16, 64], F32, kind="ExternalInput").ap()
    U_in = nc.dram_tensor("Udel", [32, 512], F32, kind="ExternalInput").ap()
    T_in = nc.dram_tensor("Tdel", [16, 512], F32, kind="ExternalInput").ap()
    id_in = nc.dram_tensor("ident", [128, 128], BF16, kind="ExternalInput").ap()
    orow_in = nc.dram_tensor("orow", [1, 128], BF16, kind="ExternalInput").ap()
    mrow_in = nc.dram_tensor("morow", [1, 128], BF16, kind="ExternalInput").ap()
    ocol_in = nc.dram_tensor("ocol", [128, 1], BF16, kind="ExternalInput").ap()
    out_d = nc.dram_tensor("out", [n_seq * S, D], F32, kind="ExternalOutput").ap()

    x1d = nc.dram_tensor("x1d", [n_seq * S, D], F32, kind="Internal").ap()
    xn2d = nc.dram_tensor("xn2d", [8, 128, n_seq * S], BF16, kind="Internal").ap()
    qd = nc.dram_tensor("qd", [n_seq, 8, 128, S], BF16, kind="Internal").ap()
    kd = nc.dram_tensor("kd", [n_seq, 8, 128, S], BF16, kind="Internal").ap()
    vd = nc.dram_tensor("vd", [n_seq, NT, 128, D], BF16, kind="Internal").ap()
    od = nc.dram_tensor("od", [n_seq * S, D], BF16, kind="Internal").ap()
    opd = nc.dram_tensor("opd", [S, D], F32, kind="Internal").ap()

    with ExitStack() as ctx:
        def sb(name, shape, dt):
            return ctx.enter_context(nc.sbuf_tensor(name, shape, dt))

        bigbuf = sb("bigbuf", [128, 32768], BF16)
        Wc = sb("Wc", [128, 8 * D], BF16)
        wpT = sb("wpTb", [128, 8 * D], BF16)
        wstage = sb("wstage", [128, 3072], F32)
        xt = [sb(f"xt{i}", [128, D], F32) for i in range(2)]
        x1t = [sb(f"x1t{i}", [128, D], F32) for i in range(2)]
        x2t = [sb(f"x2t{i}", [128, D], F32) for i in range(2)]
        ott = [sb(f"ott{i}", [128, D], F32) for i in range(2)]
        tbf = [sb(f"tbf{i}", [128, D], BF16) for i in range(2)]
        xc = [sb(f"xc{i}", [128, D], BF16) for i in range(2)]
        xnf = [sb(f"xnf{i}", [128, D], BF16) for i in range(2)]
        wide = [sb(f"wide{i}", [128, 4096], BF16) for i in range(4)]
        sA_stats = [sb(f"sAst{i}", [128, 12], F32) for i in range(2)]
        sA_mv = [sb(f"sAmv{i}", [128, 2], F32) for i in range(2)]
        sA_rstd = [sb(f"sArs{i}", [128, 1], F32) for i in range(2)]
        sA_nm = [sb(f"sAnm{i}", [128, 1], F32) for i in range(2)]
        sA_nmb = [sb(f"sAnb{i}", [128, 1], BF16) for i in range(2)]
        sA_diag = [sb(f"sAdg{i}", [128, 128], BF16) for i in range(2)]
        rk2 = [sb(f"rk2{i}", [2, 128], BF16) for i in range(2)]
        PT = [sb(f"PT{i}", [128, 256], BF16) for i in range(8)]
        stats = [sb(f"stats{i}", [128, 12], F32) for i in range(2)]
        mv = [sb(f"mv{i}", [128, 2], F32) for i in range(2)]
        negm = [sb(f"negm{i}", [128, 1], F32) for i in range(2)]
        nmb = [sb(f"nmb{i}", [128, 1], BF16) for i in range(2)]
        nmrow = [sb(f"nmrow{i}", [1, 512], BF16) for i in range(2)]
        rstd = [sb(f"rstd{i}", [128, 1], F32) for i in range(2)]
        diag = [sb(f"diag{i}", [128, 128], BF16) for i in range(2)]
        ident = sb("identb", [128, 128], BF16)
        orow = sb("orowb", [1, 128], BF16)
        morow = sb("morowb", [1, 128], BF16)
        ocol = sb("ocolb", [128, 1], BF16)
        bpj_f = sb("bpjf", [1, D], F32)
        bfn_f = sb("bfnf", [1, D], F32)
        bpj_b = sb("bpjb", [1, D], BF16)
        bfn_b = sb("bfnb", [2, D], BF16)
        sAT = [xt[1][0:32, 0:128], xt[1][0:32, 128:256]]
        sBT = [xt[1][0:16, 256:320], xt[1][0:16, 320:384]]
        sU = xt[0][0:32, 0:512]
        sT = xt[0][0:16, 512:1024]
        sums = sb("sums", [128, 2 * NQG], F32)
        rsum = sb("rsum", [128, 2], F32)
        epsb = sb("epsb", [128, 1], F32)
        xcA = [wstage[:, 1536:2048].bitcast(BF16), wstage[:, 2048:2560].bitcast(BF16)]
        nmrowA = [wstage[0:1, 2560:2816].bitcast(BF16),
                  wstage[0:1, 2816:3072].bitcast(BF16)]

        pb = [ctx.enter_context(nc.psum_tensor(f"pb{i}", [128, 512], F32))
              for i in range(8)]

        K = Kern(nc, ctx)
        block = ctx.enter_context(nc.Block())

        def WS(c0, c1):
            return [f"ws{i}" for i in range(c0 // 512, (c1 + 511) // 512)]

        def ld(queue, reads, writes, dst, src, transpose=False):
            if transpose:
                K.dma(queue, reads, writes,
                      lambda e, d=dst, s=src: e.dma_start_transpose(d, s))
            else:
                K.dma(queue, reads, writes,
                      lambda e, d=dst, s=src: e.dma_start(out=d, in_=s))

        K.op("ve", [], ["epsb"], lambda e: e.memset(epsb[:], float(EPS)))
        ld("sp", [], ["ident"], ident[:], id_in)
        ld("sp", [], ["orow"], orow[:], orow_in)
        ld("sp", [], ["morow"], morow[:], mrow_in)
        ld("sp", [], ["ocol"], ocol[:], ocol_in)
        ld("sp", [], ["bpj_f"], bpj_f[:], bpj_in)
        ld("sp", [], ["bfn_f"], bfn_f[:], bfn_in)
        K.op("act", ["bpj_f"], ["bpj_b"],
             lambda e: e.copy(bpj_b[:], bpj_f[:]))
        K.op("act", ["bfn_f"], ["bfn_b"],
             lambda e: e.mul(bfn_b[0:1, :], bfn_f[:], 0.5))

        ld("sp", [], ["xt1"], sAT[0], atR_in)
        ld("sp", [], ["xt1"], sAT[1], atI_in)
        ld("sp", [], ["xt1"], sBT[0], btR_in)
        ld("sp", [], ["xt1"], sBT[1], btI_in)
        ld("sp", [], ["xt0"], sU, U_in)
        ld("sp", [], ["xt0"], sT, T_in)


        def wv(ri, a, b):
            return wide[2 + ri][:, 2 * a:2 * b].bitcast(F32)

        for ch in range(4):
            for ri in range(2):
                atn, btn = "xt1", "xt1"
                pY, pZ = pb[2 * ri], pb[2 * ri + 1]
                pYn, pZn = f"pb{2 * ri}", f"pb{2 * ri + 1}"
                for c in range(4):
                    K.op("pe", ["xt0", atn], [pYn],
                         lambda e, c=c, ch=ch, ri=ri, pY=pY:
                         e.matmul(pY[:, c * 32:(c + 1) * 32],
                                  sU[:, ch * 128:(ch + 1) * 128],
                                  sAT[ri][:, c * 32:(c + 1) * 32],
                                  start=True, stop=True))
                    K.op("pe", ["xt0", btn], [pZn],
                         lambda e, c=c, ch=ch, ri=ri, pZ=pZ:
                         e.matmul(pZ[:, c * 16:(c + 1) * 16],
                                  sT[:, ch * 128:(ch + 1) * 128],
                                  sBT[ri][:, c * 16:(c + 1) * 16],
                                  start=True, stop=True))
                wsn_l = [f"wide{2 + ri}"]
                K.op("ve", [pYn], wsn_l,
                     lambda e, ri=ri, pY=pY:
                     e.tensor_copy(wv(ri, 0, 128), pY[:, 0:128]))
                K.op("ve", [pZn], wsn_l,
                     lambda e, ri=ri, pZ=pZ:
                     e.tensor_copy(wv(ri, 128, 192), pZ[:, 0:64]))
                for c in range(4):
                    K.op("ve", wsn_l, wsn_l,
                         lambda e, ri=ri, c=c:
                         e.tensor_tensor(
                             (wv(ri, 256, 768) if c == 0 else wv(ri, 768, 1280))
                             .rearrange("p (a b) -> p a b", a=32),
                             wv(ri, c * 32, (c + 1) * 32)
                             .unsqueeze(2).broadcast_to([128, 32, 16]),
                             wv(ri, 128 + c * 16, 128 + (c + 1) * 16)
                             .unsqueeze(1).broadcast_to([128, 32, 16]),
                             ALU.mult))
                    if c > 0:
                        K.op("ve", wsn_l, wsn_l,
                             lambda e, ri=ri:
                             e.tensor_add(wv(ri, 256, 768),
                                          wv(ri, 256, 768),
                                          wv(ri, 768, 1280)))
                if ri == 0:
                    K.op("act", wsn_l, ["Wc"],
                         lambda e, ch=ch, ri=ri:
                         e.mul(Wc[:, ch * D + 0:ch * D + 512],
                               wv(ri, 256, 768), 0.5))
                    K.op("act", wsn_l, ["Wc"],
                         lambda e, ch=ch, ri=ri:
                         e.mul(Wc[:, (4 + ch) * D + 512:(4 + ch) * D + 1024],
                               wv(ri, 256, 768), 0.5))
                else:
                    K.op("act", wsn_l, ["Wc"],
                         lambda e, ch=ch, ri=ri:
                         e.mul(Wc[:, ch * D + 512:ch * D + 1024],
                               wv(ri, 256, 768), 0.5))
                    K.op("act", wsn_l, ["Wc"],
                         lambda e, ch=ch, ri=ri:
                         e.mul(Wc[:, (4 + ch) * D + 0:(4 + ch) * D + 512],
                               wv(ri, 256, 768), -0.5))
                if ri == 1:
                    for fc in (ch, 4 + ch):
                        for half_ in range(2):
                            K.op("pe", ["Wc", "ocol"], [f"pb{4 + half_}"],
                                 lambda e, fc=fc, half_=half_, ch=ch:
                                 e.matmul(pb[4 + half_][0:1, 0:512], ocol[:],
                                          Wc[:, fc * D + half_ * 512:
                                             fc * D + half_ * 512 + 512],
                                          start=(ch == 0 and fc == ch),
                                          stop=(ch == 3 and fc == 4 + ch)))

        for half_ in range(2):
            K.op("act", [f"pb{4 + half_}"], ["ws5"],
                 lambda e, half_=half_:
                 e.mul(nmrowA[0][0:1, 0:512], pb[4 + half_][0:1, 0:512], -1.0))
            K.dma("gp", ["ws5"], ["bfn_b"],
                  lambda e, half_=half_:
                  e.dma_start(out=bfn_b[1:2, half_ * 512:half_ * 512 + 512],
                              in_=nmrowA[0][0:1, 0:512]))
        for par in range(2):
            K.op("ve", [], [f"rk2{par}"],
                 lambda e, par=par: e.memset(rk2[par][0:1, :], 1.0))

        def layernormA(par, src_t, src_n):
            sn, mn, rn = f"sAst{par}", f"sAmv{par}", f"sArs{par}"
            xn_ = "ws3" if par == 0 else "ws4"
            K.op("ve", [src_n], [sn],
                 lambda e: e.bn_stats(sA_stats[par][:, 0:6], src_t[:, 0:512]))
            K.op("ve", [src_n], [sn],
                 lambda e: e.bn_stats(sA_stats[par][:, 6:12], src_t[:, 512:1024]))
            K.op("ve", [sn], [mn],
                 lambda e: e.bn_aggr(sA_mv[par][:], sA_stats[par][:]))
            K.op("act", [mn, "epsb"], [rn],
                 lambda e: e.activation(sA_rstd[par][:], sA_mv[par][:, 1:2],
                                        AF.Sqrt, bias=epsb[:]))
            K.op("ve", [rn], [rn],
                 lambda e: e.reciprocal(sA_rstd[par][:], sA_rstd[par][:]))
            K.op("act", [src_n], [xn_],
                 lambda e: e.copy(xcA[par][:], src_t[:]))
            K.op("ve", ["ident", rn], [f"sAdg{par}"],
                 lambda e: e.tensor_scalar_mul(sA_diag[par][:], ident[:],
                                               sA_rstd[par][:]))
            K.op("ve", [mn, rn], [f"sAnb{par}"],
                 lambda e: e.tensor_mul(sA_nmb[par][:], sA_mv[par][:, 0:1],
                                        sA_rstd[par][:]))
            prow = pb[7][0:1, 0:64].bitcast(BF16)
            K.op("pe", [f"sAnb{par}", "ident"], ["pb7"],
                 lambda e, prow=prow:
                 e.transpose(prow, sA_nmb[par][:, 0:1], ident[:]))
            K.op("ve", ["pb7"], ["ws5"],
                 lambda e, prow=prow:
                 e.tensor_copy(nmrowA[par][0:1, 0:128], prow))
            K.dma("gp", ["ws5"], [f"rk2{par}"],
                  lambda e: e.dma_start(out=rk2[par][1:2, :],
                                        in_=nmrowA[par][0:1, 0:128]))

        def ln_transposeA(par, pbank_a, pbank_b, dst, dstA, dstB):
            pa, pan = pb[pbank_a], f"pb{pbank_a}"
            pc, pcn = pb[pbank_b], f"pb{pbank_b}"
            xn_ = "ws3" if par == 0 else "ws4"
            for fc in range(4):
                K.op("pe", [xn_, f"sAdg{par}"], [pan],
                     lambda e, fc=fc:
                     e.matmul(pa[:, (fc % 4) * 128:(fc % 4) * 128 + 128],
                              xcA[par][:, fc * 128:(fc + 1) * 128],
                              sA_diag[par][:], start=(fc % 4 == 0),
                              stop=(fc % 4 == 3), skip_group_check=True))
            K.op("act", [pan], [dstA],
                 lambda e: e.copy(dst[:, 0:512], pa[:]))
            for fc in range(4, 8):
                K.op("pe", [xn_, f"sAdg{par}"], [pcn],
                     lambda e, fc=fc:
                     e.matmul(pc[:, (fc % 4) * 128:(fc % 4) * 128 + 128],
                              xcA[par][:, fc * 128:(fc + 1) * 128],
                              sA_diag[par][:], start=(fc % 4 == 0),
                              stop=(fc % 4 == 3), skip_group_check=True))
            K.op("act", [pcn], [dstB],
                 lambda e: e.copy(dst[:, 512:1024], pc[:]))

        def layernorm(par, src_t, src_n):
            sn, mn, nn_, rn, dn = (f"stats{par}", f"mv{par}", f"negm{par}",
                                   f"rstd{par}", f"diag{par}")
            K.op("ve", [src_n], [sn],
                 lambda e: e.bn_stats(stats[par][:, 0:6], src_t[:, 0:512]))
            K.op("ve", [src_n], [sn],
                 lambda e: e.bn_stats(stats[par][:, 6:12], src_t[:, 512:1024]))
            K.op("ve", [sn], [mn],
                 lambda e: e.bn_aggr(mv[par][:], stats[par][:]))
            K.op("act", [mn, "epsb"], [rn],
                 lambda e: e.activation(rstd[par][:], mv[par][:, 1:2], AF.Sqrt,
                                        bias=epsb[:]))
            K.op("ve", [rn], [rn],
                 lambda e: e.reciprocal(rstd[par][:], rstd[par][:]))
            K.op("act", [src_n], [f"xc{par}"],
                 lambda e: e.copy(xc[par][:], src_t[:]))
            K.op("ve", ["ident", rn], [dn],
                 lambda e: e.tensor_scalar_mul(diag[par][:], ident[:],
                                               rstd[par][:]))
            K.op("ve", [mn, rn], [f"nmb{par}"],
                 lambda e: e.tensor_mul(nmb[par][:], mv[par][:, 0:1],
                                        rstd[par][:]))
            prow = pb[7][0:1, 0:64].bitcast(BF16)
            K.op("pe", [f"nmb{par}", "ident"], ["pb7"],
                 lambda e, prow=prow:
                 e.transpose(prow, nmb[par][:, 0:1], ident[:]))
            K.op("ve", ["pb7"], [f"nmrow{par}"],
                 lambda e, prow=prow:
                 e.tensor_copy(
                     nmrow[par][0:1, :].rearrange("o (f t) -> o f t", f=4),
                     prow.unsqueeze(1).broadcast_to([1, 4, 128])))

        def ln_transpose(par, pbank_a, pbank_b, dst, dstA, dstB):
            pa, pan = pb[pbank_a], f"pb{pbank_a}"
            pc, pcn = pb[pbank_b], f"pb{pbank_b}"
            for fc in range(4):
                K.op("pe", [f"xc{par}", f"diag{par}"], [pan],
                     lambda e, fc=fc:
                     e.matmul(pa[:, (fc % 4) * 128:(fc % 4) * 128 + 128],
                              xc[par][:, fc * 128:(fc + 1) * 128],
                              diag[par][:], start=(fc % 4 == 0), stop=False,
                              skip_group_check=True))
            K.op("pe", ["morow", f"nmrow{par}"], [pan],
                 lambda e:
                 e.matmul(pa[:], morow[:], nmrow[par][0:1, :],
                          start=False, stop=True, skip_group_check=True))
            K.op("act", [pan], [dstA],
                 lambda e: e.copy(dst[:, 0:512], pa[:]))
            for fc in range(4, 8):
                K.op("pe", [f"xc{par}", f"diag{par}"], [pcn],
                     lambda e, fc=fc:
                     e.matmul(pc[:, (fc % 4) * 128:(fc % 4) * 128 + 128],
                              xc[par][:, fc * 128:(fc + 1) * 128],
                              diag[par][:], start=(fc % 4 == 0), stop=False,
                              skip_group_check=True))
            K.op("pe", ["morow", f"nmrow{par}"], [pcn],
                 lambda e:
                 e.matmul(pc[:], morow[:], nmrow[par][0:1, :],
                          start=False, stop=True, skip_group_check=True))
            K.op("act", [pcn], [dstB],
                 lambda e: e.copy(dst[:, 512:1024], pc[:]))

        def ffn(par, pbank0, pbank1):
            p0, p0n = pb[pbank0], f"pb{pbank0}"
            p1, p1n = pb[pbank1], f"pb{pbank1}"
            for half_, (p, pn) in enumerate(((p0, p0n), (p1, p1n))):
                for fc in range(8):
                    xr = f"xnfA{par}" if fc < 4 else f"xnfB{par}"
                    K.op("pe", [xr, "Wc"], [pn],
                         lambda e, fc=fc, p=p, half_=half_:
                         e.matmul(p[:], xnf[par][:, fc * 128:(fc + 1) * 128],
                                  Wc[:, fc * D + half_ * 512:fc * D + half_ * 512 + 512],
                                  start=(fc == 0), stop=False))
                K.op("pe", [f"rk2{par}", "bfn_b"], [pn],
                     lambda e, p=p, half_=half_:
                     e.matmul(p[:], rk2[par][0:2, :],
                              bfn_b[0:2, half_ * 512:half_ * 512 + 512],
                              start=False, stop=True))

        def p1b_load(s, g):
            par = g % 2
            ld("gp", [f"xn2d{s}t{4 * g + i}" for i in range(4)], [f"wide{par}"],
               wide[par][:].rearrange("p (fc t) -> p fc t", fc=8),
               xn2d[:, :, s * S + g * 512:s * S + (g + 1) * 512]
               .rearrange("fc p t -> p fc t"))

        def pass1a(s, first=False):
            def load(t):
                par = t % 2
                rows = slice(s * S + t * 128, s * S + (t + 1) * 128)
                ld("sp", [], [f"xt{par}"], xt[par][:], x_in[rows, :])

            def compA1(t):
                par = t % 2
                ln_transposeA(par, 4 + par, 6, xnf[par], f"xnfA{par}", f"xnfB{par}")
                ffn(par, 2 * par, 2 * par + 1)

            def compA2(t):
                par = t % 2
                rows = slice(s * S + t * 128, s * S + (t + 1) * 128)
                xn, x1n = f"xt{par}", f"x1t{par}"
                p0, p0n = pb[2 * par], f"pb{2 * par}"
                p1, p1n = pb[2 * par + 1], f"pb{2 * par + 1}"
                K.op("ve", [p0n, xn], [x1n],
                     lambda e, par=par, p0=p0:
                     e.tensor_add(x1t[par][:, 0:512], p0[:], xt[par][:, 0:512]))
                K.op("ve", [p1n, xn], [x1n],
                     lambda e, par=par, p1=p1:
                     e.tensor_add(x1t[par][:, 512:1024], p1[:], xt[par][:, 512:1024]))
                K.dma("gp", [x1n], [f"x1d{s}"],
                      lambda e, par=par, rows=rows:
                      e.dma_start(out=x1d[rows, :], in_=x1t[par][:]))

            def chainB(t):
                par = t % 2
                layernorm(par, x1t[par], f"x1t{par}")

            def compB2(t):
                par = t % 2
                ln_transpose(par, 4 + par, 6, tbf[par], f"tbfA{par}", f"tbfB{par}")
                K.dma("gp", [f"tbfA{par}", f"tbfB{par}"], [f"xn2d{s}t{t}"],
                      lambda e, par=par, s=s, t=t:
                      e.dma_start(
                          out=xn2d[:, :, s * S + t * 128:s * S + (t + 1) * 128]
                          .rearrange("fc p t -> p fc t"),
                          in_=tbf[par][:].rearrange("p (fc t) -> p fc t", fc=8)))

            load(0)
            load(1)
            layernormA(0, xt[0], "xt0")
            layernormA(1, xt[1], "xt1")
            compA1(0)
            for t in range(NT):
                par = t % 2
                if t + 1 < NT:
                    compA1(t + 1)
                compA2(t)
                chainB(t)
                if t + 2 < NT:
                    load(t + 2)
                if t < 16:
                    ec, r0 = t // 2, (t % 2) * 64
                    K.dma("gp", [], ["bbK0a", "bbK0", "bbK1", "bbV0", "bbV1"],
                          lambda e, ec=ec, r0=r0:
                          e.dma_start(
                              out=bigbuf[r0:r0 + 64, ec * 3072:(ec + 1) * 3072],
                              in_=wqT_in[ec * 128 + r0:ec * 128 + r0 + 64, :]))
                if t in (6, 8):
                    p1b_load(s, 0 if t == 6 else 1)
                t0w = 16 if NT >= 24 else 8
                if first and t0w <= t < t0w + 8:
                    kc = t - t0w
                    K.dma("gp", [], ["wpT"],
                          lambda e, kc=kc:
                          e.dma_start(out=wpT[:, kc * 1024:(kc + 1) * 1024],
                                      in_=wpT_in[kc * 128:(kc + 1) * 128, :]))
                compB2(t)
                if t + 2 < NT:
                    layernormA(par, xt[par], f"xt{par}")

        def pass1b(s):
            bball = ["bbK0a", "bbK0", "bbK1", "bbV0", "bbV1"]

            def wq(ec, h0, w):
                return bigbuf[:, ec * 3072 + h0: ec * 3072 + h0 + w]

            def load(g):
                p1b_load(s, g)

            def comp(g):
                par = g % 2
                xfn = f"wide{par}"
                xf = wide[par]
                for qk, dram, dname in ((0, qd, "qd"), (1, kd, "kd")):
                    stg = wide[2 + qk]
                    stgn = f"wide{2 + qk}"
                    for hc in range(8):
                        pB, pBn = pb[hc % 2], f"pb{hc % 2}"
                        for ec in range(8):
                            K.op("pe", bball + [xfn], [pBn],
                                 lambda e, pB=pB, ec=ec, hc=hc, qk=qk, xf=xf:
                                 e.matmul(pB[:],
                                          wq(ec, qk * 1024 + hc * 128, 128),
                                          xf[:, ec * 512:(ec + 1) * 512],
                                          start=(ec == 0), stop=(ec == 7)))
                        if qk == 0:
                            K.op("act", [pBn], [stgn],
                                 lambda e, pB=pB, stg=stg, hc=hc:
                                 e.copy(stg[:, hc * 512:(hc + 1) * 512], pB[:]))
                        else:
                            K.op("ve", [pBn], [stgn],
                                 lambda e, pB=pB, stg=stg, hc=hc:
                                 e.tensor_copy(stg[:, hc * 512:(hc + 1) * 512], pB[:]))
                    K.dma("sp", [stgn], [f"{dname}{s}g{g}"],
                          lambda e, dram=dram, stg=stg, s=s, g=g:
                          e.dma_start(
                              out=dram[s, :, :, g * 512:(g + 1) * 512]
                              .rearrange("dc p q -> p dc q"),
                              in_=stg[:].rearrange("p (dc q) -> p dc q", dc=8)))
                for ts in range(4):
                    pV0, pV0n = pb[4 + ts % 2], f"pb{4 + ts % 2}"
                    pV1, pV1n = pb[6 + ts % 2], f"pb{6 + ts % 2}"
                    for ec in range(8):
                        K.op("pe", bball + [xfn], [pV0n],
                             lambda e, pV0=pV0, ec=ec, ts=ts, xf=xf:
                             e.matmul(pV0[:],
                                      xf[:, ec * 512 + ts * 128: ec * 512 + (ts + 1) * 128],
                                      wq(ec, 2048, 512),
                                      start=(ec == 0), stop=(ec == 7)))
                        K.op("pe", bball + [xfn], [pV1n],
                             lambda e, pV1=pV1, ec=ec, ts=ts, xf=xf:
                             e.matmul(pV1[:],
                                      xf[:, ec * 512 + ts * 128: ec * 512 + (ts + 1) * 128],
                                      wq(ec, 2560, 512),
                                      start=(ec == 0), stop=(ec == 7)))
                    qsn = f"xnfA{ts % 2}"
                    K.op("ve", [pV0n], [qsn],
                         lambda e, pV0=pV0, ts=ts:
                         e.tensor_copy(xnf[ts % 2][:, 0:512], pV0[:]))
                    K.dma("sp", [qsn], [f"vd{s}g{g}"],
                          lambda e, s=s, g=g, ts=ts:
                          e.dma_start(out=vd[s, g * 4 + ts, :, 0:512],
                                      in_=xnf[ts % 2][:, 0:512]))
                    K.op("act", [pV1n], [f"tbfA{ts % 2}"],
                         lambda e, pV1=pV1, ts=ts: e.copy(tbf[ts % 2][:, 0:512], pV1[:]))
                    K.dma("sp", [f"tbfA{ts % 2}"], [f"vd{s}g{g}"],
                          lambda e, s=s, g=g, ts=ts:
                          e.dma_start(out=vd[s, g * 4 + ts, :, 512:1024],
                                      in_=tbf[ts % 2][:, 0:512]))

            for g in range(NG):
                if 2 <= g + 1 < NG:
                    load(g + 1)
                comp(g)

        def pass2(s):
            QS = NKTH // 2

            def load_kv(half):
                kb = bigbuf[:, 0:8 * (S // 2)].rearrange("p (dc k) -> p dc k",
                                                          dc=8)
                for i in range(2):
                    k0 = half * (S // 2) + i * (S // 4)
                    gl = [f"kd{s}g{g}" for g in range(k0 // 512, (k0 + S // 4) // 512)]
                    if i == 0:
                        ld("sp", gl, ["bbK0a"], kb[:, :, 0:256],
                           kd[s, :, :, k0:k0 + 256].rearrange("dc p k -> p dc k"))
                        ld("sp", gl, ["bbK0"], kb[:, :, 256:S // 4],
                           kd[s, :, :, k0 + 256:k0 + S // 4]
                           .rearrange("dc p k -> p dc k"))
                    else:
                        ld("sp", gl, [f"bbK{i}"],
                           kb[:, :, i * (S // 4):(i + 1) * (S // 4)],
                           kd[s, :, :, k0:k0 + S // 4].rearrange("dc p k -> p dc k"))
                    gl = [f"vd{s}g{g}" for g in range(k0 // 512, (k0 + S // 4) // 512)]
                    tv0 = half * VH + i * (VH // 2)
                    ld("sp", gl, [f"bbV{i}"],
                       bigbuf[:, 16384 + i * (VH // 2) * 1024:
                              16384 + (i + 1) * (VH // 2) * 1024]
                       .rearrange("p (tv h) -> p tv h", tv=VH // 2),
                       vd[s, tv0:tv0 + VH // 2, :, :].rearrange("tv p h -> p tv h"))

            def load_q(half, qg):
                par = qg % 2
                ld("sp", [f"qd{s}g{qg // 2}"], [f"wide{2 + par}"],
                   wide[2 + par][:, 0:2048].rearrange("p (dc q) -> p dc q", dc=8),
                   qd[s, :, :, qg * 256:(qg + 1) * 256]
                   .rearrange("dc p q -> p dc q"))

            def kh(dc, kt):
                o = dc * (S // 2) + kt * 128
                return bigbuf[:, o:o + 128]

            def vh(tv, h0):
                o = 16384 + tv * 1024 + h0
                return bigbuf[:, o:o + 512]

            def qgroup(half, qg):
                par = qg % 2
                qtn = f"wide{2 + par}"
                qt = wide[2 + par]

                def scores(kt):
                    pS, pSn = pb[kt % 2], f"pb{kt % 2}"
                    ptn = f"PT{kt % 8}"
                    pt = PT[kt % 8]
                    for dc in range(8):
                        K.op("pe", ["bbK0a" if kt < 2 else f"bbK{kt // QS}",
                                    qtn], [pSn],
                             lambda e, pS=pS, dc=dc, kt=kt, qt=qt:
                             e.matmul(pS[:, 0:256], kh(dc, kt),
                                      qt[:, dc * 256:(dc + 1) * 256],
                                      start=(dc == 0), stop=(dc == 7)))
                    K.op("act", [pSn], [ptn],
                         lambda e, pS=pS, pt=pt:
                         e.activation(pt[:], pS[:, 0:256], AF.Exp, scale=0.125))

                def pv(kt):
                    ptn = f"PT{kt % 8}"
                    pt = PT[kt % 8]
                    for qc in range(2):
                        for hf in range(2):
                            pO, pOn = pb[2 + 2 * qc + hf], f"pb{2 + 2 * qc + hf}"
                            K.op("pe", [ptn, f"bbV{kt // QS}"], [pOn],
                                 lambda e, pO=pO, pt=pt, qc=qc, hf=hf, kt=kt:
                                 e.matmul(pO[:], pt[:, qc * 128:(qc + 1) * 128],
                                          vh(kt, hf * 512),
                                          start=(kt == 0), stop=(kt == NKTH - 1)))
                        K.op("pe", [ptn, "ocol"], [f"pb{6 + qc}"],
                             lambda e, pt=pt, qc=qc, kt=kt:
                             e.matmul(pb[6 + qc][:, 0:1],
                                      pt[:, qc * 128:(qc + 1) * 128], ocol[:],
                                      start=(kt == 0), stop=(kt == NKTH - 1)))

                for kt in range(NKTH):
                    scores(kt)
                    if kt >= 2:
                        pv(kt - 2)
                pv(NKTH - 2)
                pv(NKTH - 1)
                if half == 0:
                    for qc in range(2):
                        K.op("ve", [f"pb{6 + qc}"], ["sums"],
                             lambda e, qg=qg, qc=qc:
                             e.tensor_copy(sums[:, 2 * qg + qc:2 * qg + qc + 1],
                                           pb[6 + qc][:, 0:1]))
                    for qc in range(2):
                        for hf in range(2):
                            o0 = qc * 1024 + hf * 512
                            if hf == 0:
                                K.op("ve", [f"pb{2 + 2 * qc + hf}"],
                                     WS(o0, o0 + 512),
                                     lambda e, qc=qc, hf=hf, o0=o0:
                                     e.tensor_copy(wstage[:, o0:o0 + 512],
                                                   pb[2 + 2 * qc + hf][:]))
                            else:
                                K.op("act", [f"pb{2 + 2 * qc + hf}"],
                                     WS(o0, o0 + 512),
                                     lambda e, qc=qc, hf=hf, o0=o0:
                                     e.copy(wstage[:, o0:o0 + 512],
                                            pb[2 + 2 * qc + hf][:]))
                        K.dma("gp", WS(qc * 1024, (qc + 1) * 1024), ["opd"],
                              lambda e, qc=qc, qg=qg:
                              e.dma_start(
                                  out=opd[qg * 256 + qc * 128:
                                          qg * 256 + (qc + 1) * 128, :],
                                  in_=wstage[:, qc * 1024:(qc + 1) * 1024]))
                else:
                    for qc in range(2):
                        K.op("ve", [f"pb{6 + qc}", "sums"], ["rsum"],
                             lambda e, qg=qg, qc=qc:
                             e.tensor_add(rsum[:, qc:qc + 1], pb[6 + qc][:, 0:1],
                                          sums[:, 2 * qg + qc:2 * qg + qc + 1]))
                    K.op("ve", ["rsum"], ["rsum"],
                         lambda e: e.reciprocal(rsum[:], rsum[:]))
                    for qc in range(2):
                        ld("sp", ["opd"], WS(qc * 1024, (qc + 1) * 1024),
                           wstage[:, qc * 1024:(qc + 1) * 1024],
                           opd[qg * 256 + qc * 128:qg * 256 + (qc + 1) * 128, :])
                        for hf in range(2):
                            o0 = qc * 1024 + hf * 512
                            K.op("ve",
                                 [f"pb{2 + 2 * qc + hf}"] + WS(o0, o0 + 512),
                                 WS(o0, o0 + 512),
                                 lambda e, qc=qc, hf=hf, o0=o0:
                                 e.tensor_add(
                                     wstage[:, o0:o0 + 512],
                                     pb[2 + 2 * qc + hf][:],
                                     wstage[:, o0:o0 + 512]))
                        K.op("ve", WS(qc * 1024, (qc + 1) * 1024) + ["rsum"],
                             [f"xnfA{qc}", f"xnfB{qc}"],
                             lambda e, qc=qc:
                             e.tensor_tensor(
                                 xnf[qc][:],
                                 wstage[:, qc * 1024:(qc + 1) * 1024],
                                 rsum[:, qc:qc + 1].broadcast_to([128, 1024]),
                                 ALU.mult))
                        K.dma("gp", [f"xnfA{qc}", f"xnfB{qc}"],
                              [f"od{s}q{qg}"],
                              lambda e, qc=qc, qg=qg, s=s:
                              e.dma_start(
                                  out=od[s * S + qg * 256 + qc * 128:
                                         s * S + qg * 256 + (qc + 1) * 128, :],
                                  in_=xnf[qc][:]))

            for half in range(2):
                load_q(half, 0)
                load_kv(half)
                for qg in range(NQG):
                    if qg + 1 < NQG:
                        load_q(half, qg + 1)
                    qgroup(half, qg)

        def pass3(s):
            x1v = x1d.rearrange("(sq h f) e -> sq h f e", sq=n_seq, f=CF)
            outv = out_d.rearrange("(sq h f) e -> sq h f e", sq=n_seq, f=CF)

            def load(mt):
                c, hc = mt // 8, mt % 8
                par = mt % 2
                ld("sp", [f"od{s}q{q}" for q in range(4 * c, 4 * c + 4)],
                   [f"tbfA{par}", f"tbfB{par}"],
                   tbf[par][:].rearrange("p (kc w) -> p kc w", kc=8),
                   od[s * S + c * 1024:s * S + (c + 1) * 1024,
                      hc * 128:(hc + 1) * 128]
                   .rearrange("(kc p) w -> p kc w", kc=8))
                ld("sp", [f"x1d{s}"], [f"x1t{par}"], x1t[par][:],
                   x1v[s, hc * 128:(hc + 1) * 128, c, :])

            def compA(mt):
                c, hc = mt // 8, mt % 8
                par = mt % 2
                ob = tbf[par]
                obn = f"tbfA{par}"
                p0, p0n = pb[2 * par], f"pb{2 * par}"
                p1, p1n = pb[2 * par + 1], f"pb{2 * par + 1}"
                for half_, (p, pn) in enumerate(((p0, p0n), (p1, p1n))):
                    for kc in range(8):
                        K.op("pe", [obn, f"tbfB{par}", "wpT"], [pn],
                             lambda e, p=p, kc=kc, ob=ob, half_=half_:
                             e.matmul(p[:], ob[:, kc * 128:(kc + 1) * 128],
                                      wpT[:, kc * D + half_ * 512:
                                          kc * D + half_ * 512 + 512],
                                      start=(kc == 0), stop=False))
                    K.op("pe", ["orow", "bpj_b"], [pn],
                         lambda e, p=p, half_=half_:
                         e.matmul(p[:], orow[:],
                                  bpj_b[0:1, half_ * 512:half_ * 512 + 512],
                                  start=False, stop=True))
            def compA2(mt):
                par = mt % 2
                p0, p0n = pb[2 * par], f"pb{2 * par}"
                p1, p1n = pb[2 * par + 1], f"pb{2 * par + 1}"
                K.op("ve", [p0n, f"x1t{par}"], [f"x2t{par}"],
                     lambda e, par=par, p0=p0:
                     e.tensor_add(x2t[par][:, 0:512], p0[:], x1t[par][:, 0:512]))
                K.op("ve", [p1n, f"x1t{par}"], [f"x2t{par}"],
                     lambda e, par=par, p1=p1:
                     e.tensor_add(x2t[par][:, 512:1024], p1[:],
                                  x1t[par][:, 512:1024]))

            def compB(mt):
                c, hc = mt // 8, mt % 8
                par = mt % 2
                ln_transposeA(par, 4 + par, 6, xnf[par], f"xnfA{par}", f"xnfB{par}")
                ffn(par, 2 * par, 2 * par + 1)
                K.op("ve", [f"pb{2 * par}", f"x2t{par}"], [f"ott{par}"],
                     lambda e, par=par:
                     e.tensor_add(ott[par][:, 0:512], pb[2 * par][:],
                                  x2t[par][:, 0:512]))
                K.op("ve", [f"pb{2 * par + 1}", f"x2t{par}"], [f"ott{par}"],
                     lambda e, par=par:
                     e.tensor_add(ott[par][:, 512:1024], pb[2 * par + 1][:],
                                  x2t[par][:, 512:1024]))
                K.dma("gp", [f"ott{par}"], ["outd"],
                      lambda e, par=par, s=s, c=c, hc=hc:
                      e.dma_start(out=outv[s, hc * 128:(hc + 1) * 128, c, :],
                                  in_=ott[par][:]))

            load(0)
            load(1)
            compA(0)
            for mt in range(NMT):
                par = mt % 2
                if mt + 1 < NMT:
                    compA(mt + 1)
                compA2(mt)
                layernormA(par, x2t[par], f"x2t{par}")
                if mt + 2 < NMT:
                    load(mt + 2)
                compB(mt)

        for s in range(n_seq):
            pass1a(s, first=(s == 0))
            pass1b(s)
            pass2(s)
            pass3(s)

        final_counts = [(f"gp{i}", K.cnt[f"gp{i}"]) for i in range(4)
                        if K.cnt[f"gp{i}"] > 0]

        def fin(e, fc=final_counts, sems=K.sems):
            for s, v in fc:
                e.wait_ge(sems[s], v)
            return e.engine_nop()

        K.op("po", [], [], fin)

        K.finalize(block)

    return nc


def make_weight_inputs(w_qkv, w_proj, b_proj, G1_R, G2_R, G1_I, G2_I,
                       bias_real, bias_imag):
    f32 = np.float32
    bf16 = ml_dtypes.bfloat16
    U = np.zeros((32, 512), dtype=f32)
    T = np.zeros((16, 512), dtype=f32)
    for p in range(512):
        U[p // 16, p] = 1.0
        T[p % 16, p] = 1.0
    im = {
        "wqT": np.ascontiguousarray(np.asarray(w_qkv, dtype=f32).T),
        "wpT": np.ascontiguousarray(np.asarray(w_proj, dtype=f32).T),
        "bpj": np.asarray(b_proj, dtype=f32).reshape(1, 1024),
        "bfn": np.concatenate([np.asarray(bias_real, dtype=f32),
                               np.asarray(bias_imag, dtype=f32)]).reshape(1, 1024),
        "atR": np.ascontiguousarray(np.transpose(
            np.asarray(G1_R, dtype=f32)[0], (2, 1, 0)).reshape(32, 128)),
        "atI": np.ascontiguousarray(np.transpose(
            np.asarray(G1_I, dtype=f32)[0], (2, 1, 0)).reshape(32, 128)),
        "btR": np.ascontiguousarray(np.transpose(
            np.asarray(G2_R, dtype=f32)[:, :, 0, :], (2, 0, 1)).reshape(16, 64)),
        "btI": np.ascontiguousarray(np.transpose(
            np.asarray(G2_I, dtype=f32)[:, :, 0, :], (2, 0, 1)).reshape(16, 64)),
        "Udel": U,
        "Tdel": T,
        "ident": np.eye(128, dtype=bf16),
        "orow": np.ones((1, 128), dtype=bf16),
        "morow": -np.ones((1, 128), dtype=bf16),
        "ocol": np.ones((128, 1), dtype=bf16),
    }
    return im


def kernel(L0_attn_x, L0_attn_w_qkv, L0_attn_w_proj, L0_attn_b_proj,
           L0_ffn_G1_R, L0_ffn_G2_R, L0_ffn_G1_I, L0_ffn_G2_I,
           L0_ffn_bias_real, L0_ffn_bias_imag):
    x = np.asarray(L0_attn_x, dtype=np.float32)
    B, S, _ = x.shape
    n_seq = B // N_CORES
    wim = make_weight_inputs(L0_attn_w_qkv, L0_attn_w_proj, L0_attn_b_proj,
                             L0_ffn_G1_R, L0_ffn_G2_R, L0_ffn_G1_I, L0_ffn_G2_I,
                             L0_ffn_bias_real, L0_ffn_bias_imag)
    in_maps = []
    for c in range(N_CORES):
        im = dict(wim)
        im["x"] = np.ascontiguousarray(
            x[c * n_seq:(c + 1) * n_seq].reshape(n_seq * S, D))
        in_maps.append(im)
    nc = build_program(n_seq=n_seq, S=S)
    res = run_bass_kernel_spmd(nc, in_maps, core_ids=list(range(N_CORES)))
    out = np.concatenate([np.asarray(r["out"]).reshape(n_seq, S, D)
                          for r in res.results], axis=0)
    return out
